# version 11
# baseline (speedup 1.0000x reference)
import numpy as np
import concourse.bass as bass
import concourse.mybir as mybir
from concourse.bass_utils import run_bass_kernel_spmd
from contextlib import ExitStack

F32 = mybir.dt.float32
AF = mybir.ActivationFunctionType
ALU = mybir.AluOpType
AX = mybir.AxisListType

T = 2048
D = 1024
H = 16
DH = 64
CH = 64
NCH = T // CH
TTA = 256
NTTA = T // TTA
CPT = TTA // CH
TTF = 512
NQ = T // TTF
FF = 4096
NE = 9
EPS = 1e-5
NLEV = 6
TAPS = []


_PTR_OPS = frozenset(("tensor_scalar", "scalar_tensor_tensor",
                      "tensor_tensor_scan", "reciprocal", "tensor_reduce",
                      "reduce_max", "reduce_sum"))


class _Op:
    __slots__ = ("eng", "fn", "dem", "deps", "cum", "ptrw", "semid")

    def __init__(self, eng, fn):
        self.eng = eng
        self.fn = fn
        self.dem = False
        self.deps = set()
        self.cum = 0
        self.ptrw = (fn is not None
                     and bool(_PTR_OPS.intersection(fn.__code__.co_names)))


N_SP_SEMS = 8


class Prog:
    ENGS = ("pe", "dve", "act", "gp", "sp")
    AMT = {"pe": 1, "dve": 1, "act": 1, "gp": 1, "sp": 16}

    def __init__(self):
        self.ops = {e: [] for e in self.ENGS}
        self.bufs = {}

    def op(self, eng, fn, r=(), w=(), pr=()):
        o = _Op(eng, fn)
        self.ops[eng].append(o)
        strict = eng in ("dve", "act", "gp")
        for b in tuple(r) + tuple(pr):
            e = self.bufs.setdefault(b, {"w": None, "r": []})
            d = e["w"]
            if d is not None and d is not o and (d.eng != eng or b in pr
                                                 or d.ptrw or strict
                                                 or d.eng in ("dve", "act",
                                                              "gp")):
                o.deps.add(d)
            if b.startswith("ps"):
                for d2 in e["r"]:
                    if d2 is not o and d2.eng != eng:
                        o.deps.add(d2)
            e["r"].append(o)
        for b in w:
            e = self.bufs.setdefault(b, {"w": None, "r": []})
            d = e["w"]
            if d is not None and d is not o and (d.eng != eng or strict
                                                 or d.eng in ("dve", "act",
                                                              "gp")):
                o.deps.add(d)
            for d2 in e["r"]:
                if d2 is not o and (d2.eng != eng or strict
                                    or d2.eng in ("dve", "act", "gp")):
                    o.deps.add(d2)
            e["w"] = o
            e["r"] = []
        return o

    def barrier(self):
        lasts = []
        for e in self.ENGS:
            if e == "sp":
                lasts.extend(x for x in self.ops[e][-N_SP_SEMS:]
                             if x.fn is not None)
            elif self.ops[e]:
                lasts.append(self.ops[e][-1])
        for e in self.ENGS:
            o = _Op(e, None)
            self.ops[e].append(o)
            for d in lasts:
                if d.eng != e:
                    o.deps.add(d)

    def finalize(self, nc, sems):
        for e in self.ENGS:
            for o in self.ops[e]:
                for d in o.deps:
                    d.dem = True
        for e in self.ENGS:
            if e == "sp":
                c = [0] * N_SP_SEMS
                i = 0
                for o in self.ops[e]:
                    if o.fn is not None:
                        o.dem = True
                        o.semid = i % N_SP_SEMS
                        c[o.semid] += 1
                        o.cum = c[o.semid]
                        i += 1
                    else:
                        o.semid = None
                        o.cum = 0
            else:
                c = 0
                for o in self.ops[e]:
                    if o.dem:
                        c += 1
                    o.cum = c
        engines = {"pe": nc.tensor, "dve": nc.vector, "act": nc.scalar,
                   "gp": nc.gpsimd, "sp": nc.sync}

        def semname(d):
            return "sp%d" % d.semid if d.eng == "sp" else d.eng

        def replay(e):
            eng = engines[e]
            waited = {}
            for o in self.ops[e]:
                tgt = {}
                for d in o.deps:
                    sn = semname(d)
                    v = d.cum * self.AMT[d.eng]
                    if tgt.get(sn, 0) < v:
                        tgt[sn] = v
                for sn in sorted(tgt):
                    v = tgt[sn]
                    if waited.get(sn, 0) < v:
                        eng.wait_ge(sems[sn], v)
                        waited[sn] = v
                instr = o.fn() if o.fn is not None else None
                if o.dem:
                    if instr is None:
                        instr = eng.nop()
                    sn = semname(o) if o.eng == "sp" else o.eng
                    instr.then_inc(sems[sn], self.AMT[e])
        return replay


ROWNAMES = ["arow", "brow", "tmpA", "tmpB", "k2row", "c1row", "c2row",
            "c3row", "Arow"]
ROWP = {n: i * TTA for i, n in enumerate(ROWNAMES)}

AR_XP = 0
AR_Y = 1024
AR_MT = 4096
AR_VH = 4608
AR_KTM = 5120
AR_COLS = 5632
AR_ACREP = 5696
AR_ACROW = 5712
AR_CPAIR = 5728
AR_SLOT = 6240

BS_PP = 0
BS_OTM = 1024
BS_STT = 1536

GT_LOG = 0
GT_GROW = 512
GT_GTM = 1024
GT_GSC = 1056
GT_GMSK = 1064
GT_GCOL = 1072
GT_GMSK2 = 1076
GT_DBG = 1088

CO_IDENT = 0
CO_LMASK = 128
CO_UMASK = 192
CO_I64S = 256
CO_ONESROW = 320
CO_ONESCOL = 448
CO_BGBB = 449
CO_WGWB = 451
CO_WRT = 707
CO_HSEL = 771
CO_TOTAL = 899


def _arena_layout():
    alloc = {}
    cur = 0

    def R(name, cols):
        nonlocal cur
        alloc[name] = (cur, cols)
        cur += cols

    R("XR", 8 * T)
    R("ST", 8 * 64)
    R("CONST", CO_TOTAL)
    R("STATR", 1536)
    R("RB", 512)
    R("XSQ", 512)
    mark = cur

    R("QKV", 24 * TTA)
    R("XLN", 8 * TTA)
    R("ROWS", len(ROWNAMES) * TTA)
    R("WSLAB", 2 * 2048)
    R("ARING", 2 * AR_SLOT)
    R("BSCR", 2048)
    R("AOUT", 2 * 1024)
    R("YT", 1024)
    attn_end = cur

    cur = mark
    R("FIN", 8 * TTF)
    R("FAC", 8 * TTF)
    R("HBUF", 32 * TTF)
    R("WUP", 2 * 1024)
    R("WDN", 2 * 1024)
    R("GT", 1280)
    R("GB", 512)
    R("DSCR", 512)
    ffn_end = cur

    return alloc, max(attn_end, ffn_end)


def build_program(n_tt=NTTA, do_scan=True, n_q=NQ, stage=99, n_exp=NE):
    alloc, arena_cols = _arena_layout()
    nc = bass.Bass("TRN2", target_bir_lowering=False)

    d_x = nc.dram_tensor("x_img", [128, 8 * T], F32, kind="ExternalInput").ap()
    d_wq = nc.dram_tensor("wq_img", [12, 128, 2048], F32, kind="ExternalInput").ap()
    d_wo = nc.dram_tensor("wo_img", [4, 128, 2048], F32, kind="ExternalInput").ap()
    d_consts = nc.dram_tensor("consts_img", [128, CO_TOTAL], F32,
                              kind="ExternalInput").ap()
    d_wup = nc.dram_tensor("wup_img", [NE, 32, 128, 1024], F32,
                           kind="ExternalInput").ap()
    d_wdn = nc.dram_tensor("wdn_img", [NE, 8, 4, 128, 1024], F32,
                           kind="ExternalInput").ap()
    d_out = nc.dram_tensor("out_img", [128, 8 * T], F32, kind="ExternalOutput").ap()

    stack = ExitStack()
    ar = stack.enter_context(nc.sbuf_tensor("arena", [128, arena_cols], F32))
    ps = [stack.enter_context(nc.psum_tensor("psb%d" % i, [128, 512], F32))
          for i in range(8)]
    sems = {e: stack.enter_context(nc.semaphore("sem_" + e))
            for e in ("pe", "dve", "act", "gp")}
    for i in range(N_SP_SEMS):
        sems["sp%d" % i] = stack.enter_context(nc.semaphore("sem_sp%d" % i))

    def A(name, p0, np_, c0, ncols):
        off, sz = alloc[name]
        assert 0 <= c0 and c0 + ncols <= sz, (name, c0, ncols, sz)
        return ar[p0:p0 + np_, off + c0: off + c0 + ncols]

    def CONST(co, p0, np_, ncols):
        return A("CONST", p0, np_, co, ncols)

    P = Prog()
    op = P.op

    def rows(plane, c0, n):
        return A("ROWS", 0, 16, ROWP[plane] + c0, n)

    def IDENT(n):
        return CONST(CO_IDENT, 0, n, n)

    ONESROW = CONST(CO_ONESROW, 0, 1, 128)
    ONESCOL = CONST(CO_ONESCOL, 0, 128, 1)

    def xr_ap(dc, tcol, n):
        return A("XR", 0, 128, dc * T + tcol, n)

    def xr_keys(dc, tcol, n):
        return tuple("xr%dg%d" % (dc, g)
                     for g in range(tcol // 128, (tcol + n - 1) // 128 + 1))

    def KT(pr, half, cl):
        return A("QKV", half * 64, 64, (8 + pr) * TTA + cl * CH, CH)

    def QT(pr, half, cl):
        return A("QKV", half * 64, 64, pr * TTA + cl * CH, CH)

    def colap(nm, half, h, so):
        i = {"colsA": 0, "colsB": 1, "colsC": 2, "colsD": 3}[nm]
        return A("ARING", half * 64, 64, so + AR_COLS + i * 16 + h, 1)

    def vt64(dst_fn, src_fn, half):
        o = half * 64
        last = None
        for bi in range(2):
            for bj in range(2):
                last = nc.vector.transpose(dst_fn(o + bj * 32, bi * 32),
                                           src_fn(o + bi * 32, bj * 32))
        return last

    op("sp", lambda: nc.sync.dma_start(out=A("CONST", 0, 128, 0, CO_TOTAL),
                                       in_=d_consts),
       w=("CONST",))
    op("sp", lambda: nc.sync.dma_start(out=A("XR", 0, 128, 0, 8 * T), in_=d_x),
       w=tuple("xr%dg%d" % (dc, g) for dc in range(8) for g in range(16)))
    op("dve", lambda: nc.vector.memset(A("ST", 0, 128, 0, 512), 0.0),
       w=tuple("st%d" % pr for pr in range(8)))
    if TAPS:
        for _tn, _tr, _tp0, _tnp, _tc0, _tnc in TAPS:
            op("dve", (lambda _tr=_tr, _tp0=_tp0, _tnp=_tnp, _tc0=_tc0,
                       _tnc=_tnc: nc.vector.memset(
                           A(_tr, _tp0, _tnp, _tc0, _tnc), 0.0)),
               w=("tapinit_%s" % _tr,))

    def ln(width, nsplit, src_full, src_half, src_keys, dst_half, dst_keys,
           resid_half=None, resid_keys=None):
        wh = width // nsplit
        for dc in range(8):
            op("pe", (lambda dc=dc: nc.tensor.matmul(
                ps[2][0:1, 0:width], ONESCOL, src_full(dc),
                start=(dc == 0), stop=(dc == 7))),
               r=src_keys(dc) + ("CONST",), w=("ps2",))
        for dc in range(8):
            for hh in range(nsplit):
                i = (dc * nsplit + hh) % 2
                xk = "xsq%d" % i
                op("dve", (lambda dc=dc, hh=hh, i=i, wh=wh:
                           nc.vector.tensor_tensor(
                               A("XSQ", 0, 128, i * 256, wh),
                               src_half(dc, hh), src_half(dc, hh), ALU.mult)),
                   r=src_keys(dc), w=(xk,))
                op("pe", (lambda dc=dc, hh=hh, i=i, wh=wh, nsplit=nsplit:
                          nc.tensor.matmul(
                              ps[3][0:1, hh * wh:(hh + 1) * wh], ONESCOL,
                              A("XSQ", 0, 128, i * 256, wh),
                              start=(dc == 0 and hh == 0),
                              stop=(dc == 7 and hh == nsplit - 1))),
                   r=(xk, "CONST"), w=("ps3",))
        op("dve", (lambda width=width: nc.vector.tensor_scalar(
            A("STATR", 0, 1, 0, width), ps[2][0:1, 0:width],
            1.0 / D, None, ALU.mult)), r=("ps2",), w=("srow_m",))
        op("dve", (lambda width=width: nc.vector.tensor_scalar(
            A("STATR", 0, 1, 512, width), ps[3][0:1, 0:width],
            1.0 / D, EPS, ALU.mult, ALU.add)), r=("ps3",), w=("srow_msq",))
        op("dve", (lambda width=width: nc.vector.tensor_tensor(
            A("STATR", 0, 1, 1024, width), A("STATR", 0, 1, 0, width),
            A("STATR", 0, 1, 0, width), ALU.mult)),
           r=("srow_m",), w=("srow_t",))
        op("dve", (lambda width=width: nc.vector.tensor_tensor(
            A("STATR", 0, 1, 512, width), A("STATR", 0, 1, 512, width),
            A("STATR", 0, 1, 1024, width), ALU.subtract)),
           r=("srow_msq", "srow_t"), w=("srow_msq",))
        op("act", (lambda width=width: nc.scalar.activation(
            A("STATR", 0, 1, 1024, width), A("STATR", 0, 1, 512, width),
            AF.Sqrt)), r=("srow_msq",), w=("srow_t",))
        op("dve", (lambda width=width: nc.vector.reciprocal(
            A("STATR", 0, 1, 512, width), A("STATR", 0, 1, 1024, width))),
           r=("srow_t",), w=("srow_msq",))
        op("dve", (lambda width=width: nc.vector.tensor_tensor(
            A("STATR", 0, 1, 0, width), A("STATR", 0, 1, 0, width),
            A("STATR", 0, 1, 512, width), ALU.mult)),
           r=("srow_m", "srow_msq"), w=("srow_m",))
        for hh in range(nsplit):
            op("pe", (lambda hh=hh, wh=wh: nc.tensor.matmul(
                ps[4][0:128, 0:wh], ONESROW,
                A("STATR", 0, 1, 512 + hh * wh, wh), start=True, stop=True)),
               r=("srow_msq", "CONST"), w=("ps4",))
            op("dve", (lambda wh=wh: nc.vector.tensor_copy(
                A("RB", 0, 128, 0, wh), ps[4][0:128, 0:wh])),
               r=("ps4",), w=("rb",))
            op("pe", (lambda hh=hh, wh=wh: nc.tensor.matmul(
                ps[4][0:128, 0:wh], ONESROW,
                A("STATR", 0, 1, hh * wh, wh), start=True, stop=True)),
               r=("srow_m", "CONST"), w=("ps4",))
            op("dve", (lambda wh=wh: nc.vector.tensor_copy(
                A("RB", 0, 128, 256, wh), ps[4][0:128, 0:wh])),
               r=("ps4",), w=("mrb",))
            for dc in range(8):
                if resid_half is None:
                    op("dve", (lambda dc=dc, hh=hh, wh=wh:
                               nc.vector.tensor_tensor(
                                   dst_half(dc, hh), src_half(dc, hh),
                                   A("RB", 0, 128, 0, wh), ALU.mult)),
                       r=src_keys(dc) + ("rb",), w=dst_keys(dc))
                    op("dve", (lambda dc=dc, hh=hh, wh=wh:
                               nc.vector.tensor_tensor(
                                   dst_half(dc, hh), dst_half(dc, hh),
                                   A("RB", 0, 128, 256, wh), ALU.subtract)),
                       r=("mrb",) + dst_keys(dc), w=dst_keys(dc))
                else:
                    op("dve", (lambda dc=dc, hh=hh, wh=wh:
                               nc.vector.tensor_tensor(
                                   src_half(dc, hh), src_half(dc, hh),
                                   A("RB", 0, 128, 0, wh), ALU.mult)),
                       r=src_keys(dc) + ("rb",), w=src_keys(dc))
                    op("dve", (lambda dc=dc, hh=hh, wh=wh:
                               nc.vector.tensor_tensor(
                                   src_half(dc, hh), src_half(dc, hh),
                                   A("RB", 0, 128, 256, wh), ALU.subtract)),
                       r=("mrb",) + src_keys(dc), w=src_keys(dc))
                    op("dve", (lambda dc=dc, hh=hh:
                               nc.vector.tensor_tensor(
                                   resid_half(dc, hh), resid_half(dc, hh),
                                   src_half(dc, hh), ALU.add)),
                       r=src_keys(dc) + resid_keys(dc), w=resid_keys(dc))

    for tt in range(n_tt):
        tcol = tt * TTA

        ln(TTA, 1,
           lambda dc, tcol=tcol: xr_ap(dc, tcol, TTA),
           lambda dc, hh, tcol=tcol: xr_ap(dc, tcol, TTA),
           lambda dc, tcol=tcol: xr_keys(dc, tcol, TTA),
           lambda dc, hh: A("XLN", 0, 128, dc * TTA, TTA),
           lambda dc: ("xln%d" % dc,))

        for part in range(2 if stage >= 2 else 0):
            bank = 5 + part
            for k in range(8):
                op("pe", (lambda part=part, k=k, bank=bank: nc.tensor.matmul(
                    ps[bank][0:16, 0:TTA],
                    CONST(CO_WGWB + k * 32 + part * 16, 0, 128, 16),
                    A("XLN", 0, 128, k * TTA, TTA),
                    start=(k == 0), stop=(k == 7))),
                   r=("CONST", "xln%d" % k), w=("ps%d" % bank,))
            plane = "arow" if part == 0 else "brow"
            op("act", (lambda part=part, bank=bank, plane=plane:
                       nc.scalar.activation(
                           rows(plane, 0, TTA), ps[bank][0:16, 0:TTA],
                           AF.Sigmoid, bias=CONST(CO_BGBB + part, 0, 16, 1))),
               r=("ps%d" % bank,), pr=("CONST",), w=("row_" + plane,))

        for dc in range(24 if stage >= 5 else 0):
            bank = dc % 2
            wsl = (dc // 2) % 2
            if dc % 2 == 0:
                s = dc // 2
                op("sp", (lambda s=s: nc.sync.dma_start(
                    out=A("WSLAB", 0, 128, (s % 2) * 2048, 2048), in_=d_wq[s])),
                   w=("wsl%d" % (s % 2),))
            for k in range(8):
                op("pe", (lambda dc=dc, k=k, bank=bank, wsl=wsl:
                          nc.tensor.matmul(
                              ps[bank][0:128, 0:TTA],
                              A("WSLAB", 0, 128,
                                wsl * 2048 + k * 256 + (dc % 2) * 128, 128),
                              A("XLN", 0, 128, k * TTA, TTA),
                              start=(k == 0), stop=(k == 7))),
                   r=("wsl%d" % wsl, "xln%d" % k), w=("ps%d" % bank,))
            op("dve", (lambda dc=dc, bank=bank: nc.vector.tensor_copy(
                A("QKV", 0, 128, dc * TTA, TTA), ps[bank][0:128, 0:TTA])),
               r=("ps%d" % bank,), w=("qkv%d" % dc,))
            if dc < 16 and stage >= 7:
                i = dc % 2
                xk = "xsq%d" % i
                op("act", (lambda dc=dc, i=i: nc.scalar.activation(
                    A("XSQ", 0, 128, i * 256, TTA),
                    A("QKV", 0, 128, dc * TTA, TTA), AF.Square)),
                   r=("qkv%d" % dc,), w=(xk,))
                nb = 4 if dc < 8 else 7
                op("pe", (lambda dc=dc, nb=nb, i=i: nc.tensor.matmul(
                    ps[nb][0:16, 0:TTA],
                    CONST(CO_HSEL + (dc % 8) * 16, 0, 128, 16),
                    A("XSQ", 0, 128, i * 256, TTA),
                    start=(dc % 8 == 0), stop=(dc % 8 == 7))),
                   r=(xk, "CONST"), w=("ps%d" % nb,))

        for cl in range(CPT if stage >= 6 else 0):
            op("dve", (lambda cl=cl: nc.vector.tensor_tensor_scan(
                rows("Arow", cl * CH, CH), rows("arow", cl * CH, CH),
                rows("arow", cl * CH, CH), 1.0, ALU.mult, ALU.bypass)),
               r=("row_arow",), w=("row_Arow",))
        if stage < 6:
            continue
        op("dve", lambda: nc.vector.tensor_scalar(
            rows("tmpA", 0, TTA), ps[7][0:16, 0:TTA], 1e-24, None, ALU.max),
           r=("ps7",), w=("row_tmpA",))
        op("dve", lambda: nc.vector.reciprocal(
            rows("k2row", 0, TTA), rows("tmpA", 0, TTA)),
           r=("row_tmpA",), w=("row_k2row",))
        op("dve", lambda: nc.vector.tensor_scalar(
            rows("tmpB", 0, TTA), ps[4][0:16, 0:TTA], 1e-24, None, ALU.max),
           r=("ps4",), w=("row_tmpB",))
        op("dve", lambda: nc.vector.reciprocal(
            rows("tmpA", 0, TTA), rows("tmpB", 0, TTA)),
           r=("row_tmpB",), w=("row_tmpA",))
        op("act", lambda: nc.scalar.activation(
            rows("c3row", 0, TTA), rows("tmpA", 0, TTA), AF.Sqrt),
           r=("row_tmpA",), w=("row_c3row",))
        op("act", lambda: nc.scalar.activation(
            rows("tmpB", 0, TTA), rows("k2row", 0, TTA), AF.Sqrt),
           r=("row_k2row",), w=("row_tmpB",))
        op("dve", lambda: nc.vector.tensor_tensor(
            rows("c3row", 0, TTA), rows("c3row", 0, TTA), rows("Arow", 0, TTA),
            ALU.mult), r=("row_c3row", "row_Arow"), w=("row_c3row",))
        op("dve", lambda: nc.vector.tensor_tensor(
            rows("c1row", 0, TTA), rows("k2row", 0, TTA), rows("brow", 0, TTA),
            ALU.mult), r=("row_k2row", "row_brow"), w=("row_c1row",))
        op("dve", lambda: nc.vector.tensor_scalar(
            rows("c1row", 0, TTA), rows("c1row", 0, TTA), -1.0, None, ALU.mult),
           r=("row_c1row",), w=("row_c1row",))
        op("dve", lambda: nc.vector.tensor_tensor(
            rows("tmpB", 0, TTA), rows("tmpB", 0, TTA), rows("brow", 0, TTA),
            ALU.mult), r=("row_tmpB", "row_brow"), w=("row_tmpB",))
        op("dve", lambda: nc.vector.reciprocal(
            rows("tmpA", 0, TTA), rows("Arow", 0, TTA)),
           r=("row_Arow",), w=("row_tmpA",))
        op("dve", lambda: nc.vector.tensor_tensor(
            rows("c2row", 0, TTA), rows("tmpA", 0, TTA), rows("tmpB", 0, TTA),
            ALU.mult), r=("row_tmpA", "row_tmpB"), w=("row_c2row",))

        for cl in range(CPT if do_scan else 0):
            c = tt * CPT + cl
            slot = c % 2
            so = slot * AR_SLOT
            ck = "%d" % slot

            pairsq = [("c1row", "c2row", "colsA"), ("c2row", "c1row", "colsB"),
                      ("c3row", "Arow", "colsC"), ("Arow", "c3row", "colsD")]
            for i, (q0, q1, nm) in enumerate(pairsq):
                for j, qq in enumerate((q0, q1)):
                    op("dve", (lambda i=i, j=j, qq=qq, cl=cl, so=so:
                               nc.vector.tensor_copy(
                                   A("ARING", 0, 16,
                                     so + AR_CPAIR + i * 128 + j * 64, CH),
                                   rows(qq, cl * CH, CH))),
                       r=("row_" + qq,), w=("cp%s_%d" % (ck, i),))
            for i, (q0, q1, nm) in enumerate(pairsq):
                tb = 2 if i % 2 == 0 else 3
                op("pe", (lambda i=i, so=so, tb=tb: nc.tensor.transpose(
                    ps[tb][0:128, 0:16],
                    A("ARING", 0, 16, so + AR_CPAIR + i * 128, 128),
                    IDENT(16))),
                   r=("cp%s_%d" % (ck, i), "CONST"), w=("ps%d" % tb,))
                op("dve", (lambda i=i, nm=nm, so=so, tb=tb: nc.vector.tensor_copy(
                    A("ARING", 0, 128, so + AR_COLS + i * 16, 16),
                    ps[tb][0:128, 0:16])),
                   r=("ps%d" % tb,), w=(nm + ck,))
            op("pe", (lambda cl=cl: nc.tensor.transpose(
                ps[3][0:1, 0:16], rows("Arow", cl * CH + CH - 1, 1), IDENT(16))),
               r=("row_Arow", "CONST"), w=("ps3",))
            op("dve", (lambda so=so: nc.vector.tensor_copy(
                A("ARING", 0, 1, so + AR_ACROW, 16), ps[3][0:1, 0:16])),
               r=("ps3",), w=("acrow" + ck,))
            op("pe", (lambda so=so: nc.tensor.matmul(
                ps[3][0:128, 0:16], ONESROW,
                A("ARING", 0, 1, so + AR_ACROW, 16), start=True, stop=True)),
               r=("acrow" + ck, "CONST"), w=("ps3",))
            op("dve", (lambda so=so: nc.vector.tensor_copy(
                A("ARING", 0, 128, so + AR_ACREP, 16), ps[3][0:128, 0:16])),
               r=("ps3",), w=("acrep" + ck,))

            qkvk = lambda pr: ("qkv%d" % (8 + pr),)
            qkvq = lambda pr: ("qkv%d" % pr,)
            qkvv = lambda pr: ("qkv%d" % (16 + pr),)

            grot = [5, 6, 2, 3]
            for pr in range(8):
                gb = grot[pr % 4]
                for half in range(2):
                    op("pe", (lambda pr=pr, half=half, gb=gb, cl=cl:
                              nc.tensor.matmul(
                                  ps[gb][half * 64:half * 64 + 64, 0:64],
                                  KT(pr, half, cl), KT(pr, half, cl),
                                  start=True, stop=True)),
                       r=qkvk(pr), w=("ps%d" % gb,))
                for half in range(2):
                    h = 2 * pr + half
                    cn = "colsA" if half == 0 else "colsB"
                    op("dve", (lambda pr=pr, half=half, h=h, cn=cn, so=so, gb=gb:
                               nc.vector.scalar_tensor_tensor(
                                   A("ARING", half * 64, 64,
                                     so + AR_XP + pr * 128, 64),
                                   ps[gb][half * 64:half * 64 + 64, 0:64],
                                   colap(cn, half, h, so),
                                   CONST(CO_LMASK, half * 64, 64, 64),
                                   ALU.mult, ALU.mult)),
                       r=("ps%d" % gb, "CONST"), pr=(cn + ck,),
                       w=("x%s_%d_0" % (ck, pr),))
            for pr in range(8):
                for half in range(2):
                    op("dve", (lambda pr=pr, half=half, so=so: vt64(
                        lambda p0, f0, pr=pr, so=so: A(
                            "ARING", p0, 32, so + AR_Y + pr * 384 + f0, 32),
                        lambda p0, f0, pr=pr, so=so: A(
                            "ARING", p0, 32, so + AR_XP + pr * 128 + f0, 32),
                        half)),
                       r=("x%s_%d_0" % (ck, pr),), w=("y%s_%d_0" % (ck, pr),))
            yrot = [6, 3, 4]
            xrot = [7, 2, 5]
            for l in range(1, NLEV):
                for pr in range(8):
                    xprev = "x%s_%d_%d" % (ck, pr, (l - 1) % 2)
                    yprev = "y%s_%d_%d" % (ck, pr, l - 1)
                    xo = so + AR_XP + pr * 128 + ((l - 1) % 2) * 64
                    yo = so + AR_Y + pr * 384 + (l - 1) * 64
                    yb = yrot[pr % 3]
                    for half in range(2):
                        op("pe", (lambda xo=xo, yo=yo, half=half, yb=yb:
                                  nc.tensor.matmul(
                                      ps[yb][half * 64:half * 64 + 64, 0:64],
                                      A("ARING", half * 64, 64, xo, 64),
                                      A("ARING", half * 64, 64, yo, 64),
                                      start=True, stop=True)),
                           r=(xprev, yprev), w=("ps%d" % yb,))
                    op("act", (lambda pr=pr, l=l, so=so, yb=yb: nc.scalar.copy(
                        A("ARING", 0, 128, so + AR_Y + pr * 384 + l * 64, 64),
                        ps[yb][0:128, 0:64])),
                       r=("ps%d" % yb,), w=("y%s_%d_%d" % (ck, pr, l),))
                    if l <= NLEV - 2:
                        xb = xrot[pr % 3]
                        for half in range(2):
                            op("pe", (lambda xo=xo, yo=yo, half=half, xb=xb:
                                      nc.tensor.matmul(
                                          ps[xb][half * 64:half * 64 + 64, 0:64],
                                          A("ARING", half * 64, 64, yo, 64),
                                          A("ARING", half * 64, 64, xo, 64),
                                          start=True, stop=True)),
                               r=(xprev, yprev), w=("ps%d" % xb,))
                        op("act", (lambda pr=pr, l=l, so=so, xb=xb:
                                   nc.scalar.copy(
                                       A("ARING", 0, 128,
                                         so + AR_XP + pr * 128 + (l % 2) * 64,
                                         64),
                                       ps[xb][0:128, 0:64])),
                           r=("ps%d" % xb,), w=("x%s_%d_%d" % (ck, pr, l % 2),))
            for l in range(NLEV):
                for pr in range(8):
                    op("gp", (lambda pr=pr, l=l, so=so: nc.gpsimd.tensor_tensor(
                        A("ARING", 0, 128, so + AR_Y + pr * 384 + l * 64, 64),
                        A("ARING", 0, 128, so + AR_Y + pr * 384 + l * 64, 64),
                        CONST(CO_I64S, 0, 128, 64), ALU.add)),
                       r=("y%s_%d_%d" % (ck, pr, l), "CONST"),
                       w=("y%s_%d_%d" % (ck, pr, l),))
            mrot = [5, 6]
            for pr in range(8):
                mb = mrot[pr % 2]
                for half in range(2):
                    op("pe", (lambda pr=pr, half=half, mb=mb, cl=cl:
                              nc.tensor.matmul(
                                  ps[mb][half * 64:half * 64 + 64, 0:64],
                                  KT(pr, half, cl), QT(pr, half, cl),
                                  start=True, stop=True)),
                       r=qkvk(pr) + qkvq(pr), w=("ps%d" % mb,))
                op("dve", (lambda pr=pr, so=so, mb=mb: nc.vector.tensor_tensor(
                    A("ARING", 0, 128, so + AR_MT + pr * 64, 64),
                    ps[mb][0:128, 0:64], CONST(CO_UMASK, 0, 128, 64),
                    ALU.mult)),
                   r=("ps%d" % mb, "CONST"), w=("mt%s_%d" % (ck, pr),))
            for pr in range(8):
                for half in range(2):
                    op("dve", (lambda pr=pr, half=half, so=so, cl=cl: vt64(
                        lambda p0, f0, pr=pr, so=so: A(
                            "ARING", p0, 32, so + AR_KTM + pr * 64 + f0, 32),
                        lambda p0, f0, pr=pr, cl=cl: A(
                            "QKV", p0, 32, (8 + pr) * TTA + cl * CH + f0, 32),
                        half)),
                       r=qkvk(pr), w=("ktm%s_%d" % (ck, pr),))
                for half in range(2):
                    op("dve", (lambda pr=pr, half=half, so=so, cl=cl: vt64(
                        lambda p0, f0, pr=pr, so=so: A(
                            "ARING", p0, 32, so + AR_VH + pr * 64 + f0, 32),
                        lambda p0, f0, pr=pr, cl=cl: A(
                            "QKV", p0, 32, (16 + pr) * TTA + cl * CH + f0, 32),
                        half)),
                       r=qkvv(pr), w=("vh%s_%d" % (ck, pr),))

            krot = [5, 6, 2]
            for pr in range(8):
                kb = krot[pr % 3]
                for half in range(2):
                    h = 2 * pr + half
                    cn = "colsB" if half == 0 else "colsA"
                    op("dve", (lambda pr=pr, half=half, h=h, cn=cn, so=so:
                               nc.vector.tensor_scalar(
                                   A("BSCR", half * 64, 64,
                                     BS_PP + pr * 128 + 64, 64),
                                   A("ARING", half * 64, 64,
                                     so + AR_VH + pr * 64, 64),
                                   colap(cn, half, h, so), None, ALU.mult)),
                       r=("vh%s_%d" % (ck, pr),), pr=(cn + ck,),
                       w=("pp%d_1" % pr,))
                for half in range(2):
                    op("pe", (lambda pr=pr, half=half, kb=kb, cl=cl:
                              nc.tensor.matmul(
                                  ps[kb][half * 64:half * 64 + 64, 0:64],
                                  KT(pr, half, cl),
                                  A("ST", half * 64, 64, pr * 64, 64),
                                  start=True, stop=True)),
                       r=qkvk(pr) + ("st%d" % pr,), w=("ps%d" % kb,))
                for half in range(2):
                    h = 2 * pr + half
                    cn = "colsA" if half == 0 else "colsB"
                    op("dve", (lambda pr=pr, half=half, h=h, cn=cn, so=so, kb=kb:
                               nc.vector.scalar_tensor_tensor(
                                   A("BSCR", half * 64, 64, BS_PP + pr * 128, 64),
                                   ps[kb][half * 64:half * 64 + 64, 0:64],
                                   colap(cn, half, h, so),
                                   A("BSCR", half * 64, 64,
                                     BS_PP + pr * 128 + 64, 64),
                                   ALU.mult, ALU.add)),
                       r=("ps%d" % kb, "pp%d_1" % pr), pr=(cn + ck,),
                       w=("pp%d_0" % pr,))
            prot = [6, 7]
            for l in range(NLEV):
                for pr in range(8):
                    pb = prot[pr % 2]
                    pin = BS_PP + pr * 128 + (l % 2) * 64
                    pout = BS_PP + pr * 128 + ((l + 1) % 2) * 64
                    for half in range(2):
                        op("pe", (lambda pr=pr, l=l, half=half, pin=pin,
                                  so=so, pb=pb: nc.tensor.matmul(
                                      ps[pb][half * 64:half * 64 + 64, 0:64],
                                      A("ARING", half * 64, 64,
                                        so + AR_Y + pr * 384 + l * 64, 64),
                                      A("BSCR", half * 64, 64, pin, 64),
                                      start=True, stop=True)),
                           r=("y%s_%d_%d" % (ck, pr, l), "pp%d_%d" % (pr, l % 2)),
                           w=("ps%d" % pb,))
                    op("act", (lambda pout=pout, pb=pb: nc.scalar.copy(
                        A("BSCR", 0, 128, pout, 64), ps[pb][0:128, 0:64])),
                       r=("ps%d" % pb,), w=("pp%d_%d" % (pr, (l + 1) % 2),))
            orot = [5, 2, 3]
            g_slot = (c // 2) % 2
            for pr in range(8):
                pfin = "pp%d_%d" % (pr, NLEV % 2)
                pfo = BS_PP + pr * 128 + (NLEV % 2) * 64
                ob = orot[pr % 3]
                for half in range(2):
                    op("pe", (lambda pr=pr, half=half, pfo=pfo, so=so, ob=ob:
                              nc.tensor.matmul(
                                  ps[ob][half * 64:half * 64 + 64, 64:128],
                                  A("ARING", half * 64, 64,
                                    so + AR_MT + pr * 64, 64),
                                  A("BSCR", half * 64, 64, pfo, 64),
                                  start=True, stop=False)),
                       r=("mt%s_%d" % (ck, pr), pfin), w=("ps%d" % ob,))
                    op("pe", (lambda pr=pr, half=half, ob=ob, cl=cl:
                              nc.tensor.matmul(
                                  ps[ob][half * 64:half * 64 + 64, 64:128],
                                  QT(pr, half, cl),
                                  A("ST", half * 64, 64, pr * 64, 64),
                                  start=False, stop=True)),
                       r=qkvq(pr) + ("st%d" % pr,), w=("ps%d" % ob,))
                for half in range(2):
                    h = 2 * pr + half
                    cn = "colsC" if half == 0 else "colsD"
                    op("dve", (lambda pr=pr, half=half, h=h, cn=cn, so=so, ob=ob:
                               nc.vector.tensor_scalar(
                                   A("BSCR", half * 64, 64, BS_OTM + pr * 64, 64),
                                   ps[ob][half * 64:half * 64 + 64, 64:128],
                                   colap(cn, half, h, so), None, ALU.mult)),
                       r=("ps%d" % ob,), pr=(cn + ck,), w=("otm%d" % pr,))
                acol = g_slot * 1024 + pr * 128 + (c % 2) * 64
                for half in range(2):
                    op("dve", (lambda pr=pr, half=half, acol=acol: vt64(
                        lambda p0, f0, acol=acol: A("AOUT", p0, 32,
                                                    acol + f0, 32),
                        lambda p0, f0, pr=pr: A("BSCR", p0, 32,
                                                BS_OTM + pr * 64 + f0, 32),
                        half)),
                       r=("otm%d" % pr,), w=("ao%d" % g_slot,))
                sb = [4, 7][pr % 2]
                for half in range(2):
                    op("pe", (lambda pr=pr, half=half, pfo=pfo, so=so, sb=sb:
                              nc.tensor.matmul(
                                  ps[sb][half * 64:half * 64 + 64, 64:128],
                                  A("ARING", half * 64, 64,
                                    so + AR_KTM + pr * 64, 64),
                                  A("BSCR", half * 64, 64, pfo, 64),
                                  start=True, stop=True)),
                       r=("ktm%s_%d" % (ck, pr), pfin), w=("ps%d" % sb,))
                op("dve", (lambda pr=pr, sb=sb: nc.vector.tensor_tensor(
                    A("BSCR", 0, 128, BS_STT + pr * 64, 64),
                    ps[sb][0:128, 64:128], A("ST", 0, 128, pr * 64, 64),
                    ALU.add)),
                   r=("ps%d" % sb, "st%d" % pr), w=("sttmp%d" % pr,))
                for half in range(2):
                    h = 2 * pr + half
                    op("dve", (lambda pr=pr, half=half, h=h, so=so:
                               nc.vector.tensor_scalar(
                                   A("ST", half * 64, 64, pr * 64, 64),
                                   A("BSCR", half * 64, 64,
                                     BS_STT + pr * 64, 64),
                                   A("ARING", half * 64, 64,
                                     so + AR_ACREP + h, 1),
                                   None, ALU.mult)),
                       r=("sttmp%d" % pr,), pr=("acrep" + ck,),
                       w=("st%d" % pr,))

            if c % 2 == 1:
                g = c // 2
                gs = g % 2
                gcol = g * 128
                for dco in range(8):
                    bank = dco % 2
                    wsl = (dco // 2) % 2
                    if dco % 2 == 0:
                        s = dco // 2
                        op("sp", (lambda s=s: nc.sync.dma_start(
                            out=A("WSLAB", 0, 128, (s % 2) * 2048, 2048),
                            in_=d_wo[s])),
                           w=("wsl%d" % (s % 2),))
                    for k in range(8):
                        op("pe", (lambda dco=dco, k=k, bank=bank, wsl=wsl,
                                  gs=gs: nc.tensor.matmul(
                                      ps[bank][0:128, 0:128],
                                      A("WSLAB", 0, 128,
                                        wsl * 2048 + k * 256 + (dco % 2) * 128,
                                        128),
                                      A("AOUT", 0, 128, gs * 1024 + k * 128,
                                        128),
                                      start=(k == 0), stop=(k == 7))),
                           r=("wsl%d" % wsl, "ao%d" % gs), w=("ps%d" % bank,))
                    op("dve", (lambda dco=dco, bank=bank: nc.vector.tensor_copy(
                        A("YT", 0, 128, dco * 128, 128),
                        ps[bank][0:128, 0:128])),
                       r=("ps%d" % bank,), w=("yt%d" % dco,))
                ln(128, 1,
                   lambda dc: A("YT", 0, 128, dc * 128, 128),
                   lambda dc, hh: A("YT", 0, 128, dc * 128, 128),
                   lambda dc: ("yt%d" % dc,),
                   None, None,
                   resid_half=lambda dc, hh, gcol=gcol: xr_ap(dc, gcol, 128),
                   resid_keys=lambda dc, g=g: ("xr%dg%d" % (dc, g),))

    P.barrier()

    for q in range(n_q):
        fcol = q * TTF
        qgs = tuple(range(q * 4, q * 4 + 4))

        ln(TTF, 2,
           lambda dc, fcol=fcol: xr_ap(dc, fcol, TTF),
           lambda dc, hh, fcol=fcol: xr_ap(dc, fcol + hh * 256, 256),
           lambda dc, qgs=qgs: tuple("xr%dg%d" % (dc, g) for g in qgs),
           lambda dc, hh: A("FIN", 0, 128, dc * TTF + hh * 256, 256),
           lambda dc: ("fin%d" % dc,))

        for g4 in range(4):
            for k in range(8):
                op("pe", (lambda g4=g4, k=k: nc.tensor.matmul(
                    ps[3][0:128, 0:8],
                    A("FIN", 0, 128, k * TTF + g4 * 128, 128),
                    CONST(CO_WRT + k * 8, 0, 128, 8),
                    start=(k == 0), stop=(k == 7))),
                   r=("CONST", "fin%d" % k), w=("ps3",))
            op("dve", lambda: nc.vector.tensor_copy(
                A("GT", 0, 128, GT_GMSK, 8), ps[3][0:128, 0:8]),
               r=("ps3",), w=("gmsk",))
            op("act", lambda: nc.scalar.activation(
                A("GT", 0, 128, GT_GSC, 8), A("GT", 0, 128, GT_GMSK, 8),
                AF.Exp), r=("gmsk",), w=("gsc",))
            op("dve", lambda: nc.vector.reduce_sum(
                A("GT", 0, 128, GT_GCOL + 1, 1), A("GT", 0, 128, GT_GSC, 8),
                axis=AX.X), r=("gsc",), w=("gc1",))
            op("act", lambda: nc.scalar.activation(
                A("GT", 0, 128, GT_GCOL + 2, 1), A("GT", 0, 128, GT_GCOL + 1, 1),
                AF.Ln), r=("gc1",), w=("gc2",))
            op("act", lambda: nc.scalar.activation(
                A("GT", 0, 128, GT_GCOL + 2, 1), A("GT", 0, 128, GT_GCOL + 2, 1),
                AF.Exp, scale=-1.0), r=("gc2",), w=("gc2",))
            op("dve", lambda: nc.vector.tensor_tensor(
                A("GT", 0, 128, GT_GMSK2, 4), A("GT", 0, 128, GT_GSC, 4),
                A("GT", 0, 128, GT_GSC + 4, 4), ALU.max),
               r=("gsc",), w=("gmsk2",))
            op("dve", lambda: nc.vector.tensor_tensor(
                A("GT", 0, 128, GT_GMSK2 + 4, 2), A("GT", 0, 128, GT_GMSK2, 2),
                A("GT", 0, 128, GT_GMSK2 + 2, 2), ALU.max),
               r=("gmsk2",), w=("gmsk2",))
            op("dve", lambda: nc.vector.tensor_tensor(
                A("GT", 0, 128, GT_GCOL, 1), A("GT", 0, 128, GT_GMSK2 + 4, 1),
                A("GT", 0, 128, GT_GMSK2 + 5, 1), ALU.max),
               r=("gmsk2",), w=("gc0",))
            op("dve", (lambda g4=g4: nc.vector.tensor_copy(
                A("GT", 0, 128, GT_DBG + g4 * 40, 8), A("GT", 0, 128, GT_GSC, 8))),
               r=("gsc",), w=("gdbg",))
            op("dve", (lambda g4=g4: nc.vector.tensor_copy(
                A("GT", 0, 128, GT_DBG + g4 * 40 + 8, 1), A("GT", 0, 128, GT_GCOL, 1))),
               r=("gc0",), w=("gdbg",))
            op("dve", lambda: nc.vector.tensor_scalar(
                A("GT", 0, 128, GT_GMSK, 8), A("GT", 0, 128, GT_GSC, 8),
                A("GT", 0, 128, GT_GCOL, 1), None, ALU.is_equal),
               r=("gsc",), pr=("gc0",), w=("gmsk",))
            op("dve", (lambda g4=g4: nc.vector.tensor_copy(
                A("GT", 0, 128, GT_DBG + g4 * 40 + 9, 8), A("GT", 0, 128, GT_GMSK, 8))),
               r=("gmsk",), w=("gdbg",))
            op("dve", (lambda g4=g4: nc.vector.scalar_tensor_tensor(
                A("GT", 0, 128, GT_GTM + g4 * 8, 8), A("GT", 0, 128, GT_GMSK, 8),
                -30000.0, A("GT", 0, 128, GT_GSC, 8), ALU.mult, ALU.add)),
               r=("gmsk", "gsc"), w=("gtm%d" % g4,))
            op("dve", (lambda g4=g4: nc.vector.tensor_tensor(
                A("GT", 0, 128, GT_GMSK2, 4), A("GT", 0, 128, GT_GTM + g4 * 8, 4),
                A("GT", 0, 128, GT_GTM + g4 * 8 + 4, 4), ALU.max)),
               r=("gtm%d" % g4,), w=("gmsk2",))
            op("dve", lambda: nc.vector.tensor_tensor(
                A("GT", 0, 128, GT_GMSK2 + 4, 2), A("GT", 0, 128, GT_GMSK2, 2),
                A("GT", 0, 128, GT_GMSK2 + 2, 2), ALU.max),
               r=("gmsk2",), w=("gmsk2",))
            op("dve", lambda: nc.vector.tensor_tensor(
                A("GT", 0, 128, GT_GCOL + 3, 1), A("GT", 0, 128, GT_GMSK2 + 4, 1),
                A("GT", 0, 128, GT_GMSK2 + 5, 1), ALU.max),
               r=("gmsk2",), w=("gc3",))
            op("dve", (lambda g4=g4: nc.vector.tensor_copy(
                A("GT", 0, 128, GT_DBG + g4 * 40 + 17, 8),
                A("GT", 0, 128, GT_GTM + g4 * 8, 8))),
               r=("gtm%d" % g4,), w=("gdbg",))
            op("dve", (lambda g4=g4: nc.vector.tensor_copy(
                A("GT", 0, 128, GT_DBG + g4 * 40 + 25, 1),
                A("GT", 0, 128, GT_GCOL + 3, 1))),
               r=("gc3",), w=("gdbg",))
            op("dve", lambda: nc.vector.tensor_scalar(
                A("GT", 0, 128, GT_GMSK, 8), A("GT", 0, 128, GT_GSC, 8),
                A("GT", 0, 128, GT_GCOL + 3, 1), None, ALU.is_ge),
               r=("gsc",), pr=("gc3",), w=("gmsk",))
            op("dve", (lambda g4=g4: nc.vector.tensor_copy(
                A("GT", 0, 128, GT_DBG + g4 * 40 + 26, 8),
                A("GT", 0, 128, GT_GMSK, 8))),
               r=("gmsk",), w=("gdbg",))
            op("dve", lambda: nc.vector.tensor_tensor(
                A("GT", 0, 128, GT_GSC, 8), A("GT", 0, 128, GT_GSC, 8),
                A("GT", 0, 128, GT_GMSK, 8), ALU.mult),
               r=("gsc", "gmsk"), w=("gsc",))
            op("dve", (lambda g4=g4: nc.vector.tensor_scalar(
                A("GT", 0, 128, GT_GTM + g4 * 8, 8),
                A("GT", 0, 128, GT_GSC, 8),
                A("GT", 0, 128, GT_GCOL + 2, 1), None, ALU.mult)),
               r=("gsc",), pr=("gc2",), w=("gtm%d" % g4,))

        for dc in range(8):
            op("dve", (lambda dc=dc: nc.vector.tensor_copy(
                A("FAC", 0, 128, dc * TTF, TTF),
                A("FIN", 0, 128, dc * TTF, TTF))),
               r=("fin%d" % dc,), w=("fac%d" % dc,))

        for e in range(n_exp):
            if e > 0:
                for g4 in range(4):
                    op("pe", (lambda e=e, g4=g4: nc.tensor.transpose(
                        ps[3][0:1, 0:128],
                        A("GT", 0, 128, GT_GTM + g4 * 8 + (e - 1), 1),
                        IDENT(128))),
                       r=("gtm%d" % g4, "CONST"), w=("ps3",))
                    op("dve", (lambda g4=g4: nc.vector.tensor_copy(
                        A("GT", 0, 1, GT_GROW + g4 * 128, 128),
                        ps[3][0:1, 0:128])),
                       r=("ps3",), w=("grow",))
                op("pe", lambda: nc.tensor.matmul(
                    ps[3][0:128, 0:TTF], ONESROW, A("GT", 0, 1, GT_GROW, TTF),
                    start=True, stop=True),
                   r=("grow", "CONST"), w=("ps3",))
                op("dve", lambda: nc.vector.tensor_copy(
                    A("GB", 0, 128, 0, TTF), ps[3][0:128, 0:TTF]),
                   r=("ps3",), w=("gb",))
            for fc in range(32):
                wslot = fc % 2
                bank = fc % 2
                op("sp", (lambda e=e, fc=fc, wslot=wslot: nc.sync.dma_start(
                    out=A("WUP", 0, 128, wslot * 1024, 1024),
                    in_=d_wup[e, fc])),
                   w=("wup%d" % wslot,))
                for k in range(8):
                    op("pe", (lambda fc=fc, k=k, bank=bank, wslot=wslot:
                              nc.tensor.matmul(
                                  ps[bank][0:128, 0:TTF],
                                  A("WUP", 0, 128, wslot * 1024 + k * 128, 128),
                                  A("FIN", 0, 128, k * TTF, TTF),
                                  start=(k == 0), stop=(k == 7))),
                       r=("wup%d" % wslot, "fin%d" % k), w=("ps%d" % bank,))
                op("act", (lambda fc=fc, bank=bank: nc.scalar.activation(
                    A("HBUF", 0, 128, fc * TTF, TTF), ps[bank][0:128, 0:TTF],
                    AF.Gelu)),
                   r=("ps%d" % bank,), w=("hb%d" % fc,))
            for dm in range(8):
                bank = dm % 2
                for s4 in range(4):
                    wslot = s4 % 2
                    op("sp", (lambda e=e, dm=dm, s4=s4, wslot=wslot:
                              nc.sync.dma_start(
                                  out=A("WDN", 0, 128, wslot * 1024, 1024),
                                  in_=d_wdn[e, dm, s4])),
                       w=("wdn%d" % wslot,))
                    for kk in range(8):
                        fc = s4 * 8 + kk
                        op("pe", (lambda fc=fc, kk=kk, bank=bank, wslot=wslot:
                                  nc.tensor.matmul(
                                      ps[bank][0:128, 0:TTF],
                                      A("WDN", 0, 128,
                                        wslot * 1024 + kk * 128, 128),
                                      A("HBUF", 0, 128, fc * TTF, TTF),
                                      start=(fc == 0), stop=(fc == 31))),
                           r=("wdn%d" % wslot, "hb%d" % fc),
                           w=("ps%d" % bank,))
                if e == 0:
                    op("dve", (lambda dm=dm, bank=bank: nc.vector.tensor_tensor(
                        A("FAC", 0, 128, dm * TTF, TTF),
                        A("FAC", 0, 128, dm * TTF, TTF),
                        ps[bank][0:128, 0:TTF], ALU.add)),
                       r=("ps%d" % bank, "fac%d" % dm), w=("fac%d" % dm,))
                else:
                    op("dve", (lambda dm=dm, bank=bank: nc.vector.tensor_tensor(
                        A("DSCR", 0, 128, 0, TTF), ps[bank][0:128, 0:TTF],
                        A("GB", 0, 128, 0, TTF), ALU.mult)),
                       r=("ps%d" % bank, "gb"), w=("dscr",))
                    op("dve", (lambda dm=dm: nc.vector.tensor_tensor(
                        A("FAC", 0, 128, dm * TTF, TTF),
                        A("FAC", 0, 128, dm * TTF, TTF),
                        A("DSCR", 0, 128, 0, TTF), ALU.add)),
                       r=("dscr", "fac%d" % dm), w=("fac%d" % dm,))

        ln(TTF, 2,
           lambda dc: A("FAC", 0, 128, dc * TTF, TTF),
           lambda dc, hh: A("FAC", 0, 128, dc * TTF + hh * 256, 256),
           lambda dc: ("fac%d" % dc,),
           None, None,
           resid_half=lambda dc, hh, fcol=fcol: xr_ap(dc, fcol + hh * 256, 256),
           resid_keys=lambda dc, qgs=qgs: tuple("xr%dg%d" % (dc, g)
                                                for g in qgs))
        op("sp", (lambda fcol=fcol: nc.sync.dma_start(
            out=d_out.rearrange("p (dc t) -> p dc t", dc=8)[
                :, :, fcol:fcol + TTF],
            in_=A("XR", 0, 128, 0, 8 * T).rearrange(
                "p (dc t) -> p dc t", dc=8)[:, :, fcol:fcol + TTF])),
           r=tuple("xr%dg%d" % (dc, g) for dc in range(8) for g in qgs),
           w=("OUT",))

    if n_q == 0:
        op("sp", lambda: nc.sync.dma_start(
            out=d_out, in_=A("XR", 0, 128, 0, 8 * T)),
           r=tuple("xr%dg%d" % (dc, g) for dc in range(8) for g in range(16)),
           w=("OUT",))
    if TAPS:
        P.barrier()
        for name, region, p0, npp, c0, ncols in TAPS:
            dt = nc.dram_tensor("tap_" + name, [npp, ncols], F32,
                                kind="ExternalOutput").ap()
            op("sp", (lambda dt=dt, region=region, p0=p0, npp=npp, c0=c0,
                      ncols=ncols: nc.sync.dma_start(
                          out=dt, in_=A(region, p0, npp, c0, ncols))),
               w=("OUT",))
    P.barrier()

    replay = P.finalize(nc, sems)
    with nc.Block() as block:
        @block.tensor
        def _(tensor):
            replay("pe")

        @block.vector
        def _(vector):
            replay("dve")

        @block.scalar
        def _(scalar):
            replay("act")

        @block.gpsimd
        def _(gpsimd):
            replay("gp")

        @block.sync
        def _(sync):
            replay("sp")

    stack.close()
    return nc


def _consts_img(bg, bb, Wg, Wb, router_w):
    img = np.zeros((128, CO_TOTAL), dtype=np.float32)
    img[:, CO_IDENT:CO_IDENT + 128] = np.eye(128, dtype=np.float32)
    lm = np.tril(np.ones((64, 64), np.float32), -1)
    img[0:64, CO_LMASK:CO_LMASK + 64] = lm
    img[64:128, CO_LMASK:CO_LMASK + 64] = lm
    um = np.triu(np.ones((64, 64), np.float32), 0)
    img[0:64, CO_UMASK:CO_UMASK + 64] = um
    img[64:128, CO_UMASK:CO_UMASK + 64] = um
    i64 = np.eye(64, dtype=np.float32)
    img[0:64, CO_I64S:CO_I64S + 64] = i64
    img[64:128, CO_I64S:CO_I64S + 64] = i64
    img[0, CO_ONESROW:CO_ONESROW + 128] = 1.0
    img[:, CO_ONESCOL] = 1.0
    img[0:16, CO_BGBB] = bg
    img[0:16, CO_BGBB + 1] = bb
    wgwb = np.concatenate([Wg, Wb], axis=0).T.astype(np.float32)
    img[:, CO_WGWB:CO_WGWB + 256] = wgwb.reshape(
        8, 128, 32).transpose(1, 0, 2).reshape(128, 256)
    img[:, CO_WRT:CO_WRT + 64] = router_w.astype(np.float32).reshape(
        8, 128, 8).transpose(1, 0, 2).reshape(128, 64)
    hs = np.zeros((128, 8, 16), np.float32)
    for i in range(8):
        hs[0:64, i, 2 * i] = 1.0
        hs[64:128, i, 2 * i + 1] = 1.0
    img[:, CO_HSEL:CO_HSEL + 128] = hs.reshape(128, 128)
    return img


def _w_images(Wqkv, Wout, sup, sdn, rup, rdn):
    wq = np.ascontiguousarray(Wqkv.T).reshape(8, 128, 12, 256)
    wq = np.ascontiguousarray(wq.transpose(2, 1, 0, 3)).reshape(12, 128, 2048)
    wo = np.ascontiguousarray(Wout.T).reshape(8, 128, 4, 256)
    wo = np.ascontiguousarray(wo.transpose(2, 1, 0, 3)).reshape(4, 128, 2048)
    ups = np.concatenate([sup, rup], axis=0)
    dns = np.concatenate([sdn, rdn], axis=0)
    wup = ups.reshape(9, 8, 128, 32, 128)
    wup = np.ascontiguousarray(wup.transpose(0, 3, 2, 1, 4)).reshape(
        9, 32, 128, 1024)
    wdn = dns.reshape(9, 4, 8, 128, 8, 128)
    wdn = np.ascontiguousarray(wdn.transpose(0, 4, 1, 3, 2, 5)).reshape(
        9, 8, 4, 128, 1024)
    return wq, wo, wup, wdn


def kernel(L0_attn_x, L0_attn_Wqkv, L0_attn_Wg, L0_attn_bg, L0_attn_Wb,
           L0_attn_bb, L0_attn_Wout, L0_ffn_shared_up, L0_ffn_shared_down,
           L0_ffn_routed_up, L0_ffn_routed_down, L0_ffn_router_w):
    x = np.asarray(L0_attn_x, dtype=np.float32)
    consts = _consts_img(np.asarray(L0_attn_bg, np.float32),
                         np.asarray(L0_attn_bb, np.float32),
                         np.asarray(L0_attn_Wg, np.float32),
                         np.asarray(L0_attn_Wb, np.float32),
                         np.asarray(L0_ffn_router_w, np.float32))
    wq, wo, wup, wdn = _w_images(
        np.asarray(L0_attn_Wqkv, np.float32),
        np.asarray(L0_attn_Wout, np.float32),
        np.asarray(L0_ffn_shared_up, np.float32),
        np.asarray(L0_ffn_shared_down, np.float32),
        np.asarray(L0_ffn_routed_up, np.float32),
        np.asarray(L0_ffn_routed_down, np.float32))

    nc = build_program()
    in_maps = []
    for b in range(8):
        xb = np.ascontiguousarray(x[b].T)
        x_img = np.ascontiguousarray(
            xb.reshape(8, 128, T).transpose(1, 0, 2)).reshape(128, 8 * T)
        in_maps.append({
            "x_img": x_img, "wq_img": wq, "wo_img": wo, "consts_img": consts,
            "wup_img": wup, "wdn_img": wdn,
        })
    res = run_bass_kernel_spmd(nc, in_maps, core_ids=list(range(8)))
    outs = []
    for b in range(8):
        o = res.results[b]["out_img"].reshape(128, 8, T)
        outs.append(np.ascontiguousarray(o.transpose(2, 1, 0)).reshape(T, D))
    return np.stack(outs, axis=0)
